# version 05
# baseline (speedup 1.0000x reference)
import numpy as np
import ml_dtypes
import concourse.bass as bass
import concourse.mybir as mybir
from concourse.bass_utils import run_bass_kernel_spmd
from contextlib import ExitStack

F32 = mybir.dt.float32
BF16 = mybir.dt.bfloat16
AF = mybir.ActivationFunctionType
ALU = mybir.AluOpType
AX = mybir.AxisListType.X

NCORE = 8
D = 2048
H = 16
FF = 1024
NRE = 16
NU = 18
T = 1024
DC = 16
FC = 8
SCALE = 1.0 / float(np.sqrt(128.0))
NEGC = -12.0
EPS = 1e-5


def pieces_of(w):
    c0 = w * 128
    W = 1024 - c0
    out = [(c0, min(512, W))]
    if W > 512:
        out.append((c0 + 512, W - 512))
    return out


def build_nc(debug=False):
    nc = bass.Bass("TRN2", target_bir_lowering=False, debug=False)

    def din(name, shape, dt=F32):
        return nc.dram_tensor(name, shape, dt, kind="ExternalInput").ap()

    x_d = din("x_sh", [T, D])
    wqkv_d = din("wqkv", [3 * D, D])
    wqkvb_d = din("wqkv_b", [3 * D])
    wa_d = din("wa", [D, 16])
    rg_d = din("rg", [NRE, D, FF]); rw1_d = din("rw1", [NRE, D, FF]); rw2_d = din("rw2", [NRE, FF, D])
    rgb_d = din("rgb", [NRE, FF]); rb1_d = din("rb1", [NRE, FF]); rb2_d = din("rb2", [NRE, D])
    sg_d = din("sg", [2, D, FF]); sw1_d = din("sw1", [2, D, FF]); sw2_d = din("sw2", [2, FF, D])
    sgb_d = din("sgb", [2, FF]); sb1_d = din("sb1", [2, FF]); sb2_d = din("sb2", [2, D])
    csc_d = din("cs_c", [32, T], BF16)
    css_d = din("cs_s", [32, T], BF16)
    mask_d = din("mask01", [128, 8, 128], BF16)
    ident_d = din("ident", [128, 128])
    oneh_d = din("oneh16", [128, H, 16], BF16)
    y_d = nc.dram_tensor("y", [T, D], F32, kind="ExternalOutput").ap()

    xn_dram = nc.dram_tensor("xn_dram", [T, D], BF16, kind="Internal").ap()
    wq_bf = nc.dram_tensor("wq_bf", [3 * D, D], BF16, kind="Internal").ap()
    q_dram = nc.dram_tensor("q_dram", [D, T], BF16, kind="Internal").ap()
    kc_dram = nc.dram_tensor("kc_dram", [D, T], BF16, kind="Internal").ap()
    vc_dram = nc.dram_tensor("vc_dram", [T, D], BF16, kind="Internal").ap()
    kg_dram = nc.dram_tensor("kg_dram", [NCORE * D, T], BF16, kind="Internal").ap()
    vg_dram = nc.dram_tensor("vg_dram", [NCORE * T, D], BF16, kind="Internal").ap()
    ffn_dram = nc.dram_tensor("ffn_dram", [T, D], F32, kind="Internal").ap()

    ST = {e: [] for e in ("tensor", "vector", "scalar", "sync", "gpsimd")}
    CNT = {"dS": 0, "dG": 0, "cT": 0, "cV": 0, "cA": 0, "cC": 0}
    SEM = {}

    def E(eng, method, *a, w=(), inc=None, incn=1, **kw):
        wm = {}
        for (sn, v) in w:
            if v > 0 and v > wm.get(sn, 0):
                wm[sn] = v
        wl = list(wm.items())

        def f(e, method=method, a=a, kw=kw, wl=wl, inc=inc, incn=incn):
            for (sn, v) in wl:
                e.wait_ge(SEM[sn], v)
            r = getattr(e, method)(*a, **kw)
            if inc:
                r.then_inc(SEM[inc], incn)
        ST[eng].append(f)

    def bump(sn, n=1):
        CNT[sn] += n
        return CNT[sn]

    fill = {}
    evac = {}

    def gf(*keys):
        return [evac[k] for k in keys if k in evac]

    def ge(*keys):
        return [fill[k] for k in keys if k in fill]

    with ExitStack() as st:
        def sb(name, shape, dt):
            return st.enter_context(nc.sbuf_tensor(name, shape, dt))

        ident = sb("identW", [128, 128], F32)
        csb_c = sb("csb_c", [32, T], BF16)
        csb_s = sb("csb_s", [32, T], BF16)
        mask01 = sb("mask01b", [128, 8, 128], BF16)
        biasqk = sb("biasqk", [128, 48], F32)
        biasqk_r = sb("biasqk_r", [48, 128], F32)
        oneh = sb("oneh", [128, H, 16], BF16)
        ones_f = sb("ones_f", [1, 128], F32)
        onesr_bf = sb("onesr_bf", [1, 128], BF16)
        ones2_bf = sb("ones2_bf", [2, 128], BF16)
        negc = sb("negc", [128, 1], F32)
        bcol_g = sb("bcol_g", [128, FC, NRE], F32)
        bcol_w = sb("bcol_w", [128, FC, NRE], F32)
        bcol_gs = sb("bcol_gs", [128, FC, 2], F32)
        bcol_ws = sb("bcol_ws", [128, FC, 2], F32)
        rb2_bf = sb("rb2_bf", [16, D], BF16)
        sb2_bf = sb("sb2_bf", [2, D], BF16)
        wa_bf = sb("wa_bf", [128, DC, 16], BF16)
        aff_tm = sb("aff_tm", [128, 8, 16], F32)
        comb_tm = sb("comb_tm", [128, 8, 16], F32)
        comb_bf = sb("comb_bf", [16, T], BF16)
        topk_s = sb("topk_s", [128, 8], F32)

        B32 = sb("B32", [128, 16384], F32)
        ABF = sb("ABF", [128, 48 * 1024], BF16)
        AFR = sb("AFR", [128, 6144], F32)

        out_acc = B32[:, :].rearrange("p (a c) -> p a c", a=8)
        s_acc = B32[:, :].rearrange("p (a c) -> p a c", a=16)

        def abf(off, shape):
            n = int(np.prod(shape))
            ap = ABF[:, off:off + n]
            if len(shape) == 1:
                return ap
            return ap.rearrange("p (a c) -> p a c", a=shape[0])

        xn_fm = abf(0, [16, 1024])
        wqT = [abf(16384, [16, 512]), abf(24576, [16, 512])]
        qkv_ev = [abf(32768 + i * 1024, [1024]) for i in range(4)]
        xn_bf_t = [abf(36864, [2048]), abf(38912, [2048])]
        rope_scr = abf(40960, [3, 1024])
        vbias_row = ABF[0:1, 41984:44032]

        y_sb = abf(16384, [8, 1024])
        gwg = [abf(24576, [16, 128]), abf(26624, [16, 128])]
        gww = [abf(28672, [16, 128]), abf(30720, [16, 128])]
        w2buf = abf(32768, [8, 2048])

        q_sb = abf(0, [16, 1024])
        k_h = [abf(16384, [8, 128]), abf(17408, [8, 128])]
        v_h = [abf(18432, [8, 128]), abf(19456, [8, 128])]
        p_t = [abf(20480, [8, 512]), abf(24576, [8, 512])]
        lsum_bf = abf(28672, [512])

        def afr(off, n, p=128):
            return AFR[0:p, off:off + n]

        x_t = [afr(0, 2048), afr(2048, 2048)]
        ln_scr = afr(4096, 2048)
        rgb_sb = afr(0, 1024, 16)
        rb1_sb = afr(1024, 1024, 16)
        sgb_sb = afr(2048, 1024, 2)
        sb1_sb = afr(3072, 1024, 2)
        aff_fm = afr(4096, 1024, 16)
        t_g = afr(0, 1024)
        t_w = afr(1024, 1024)
        c_bc = afr(2048, 1024)
        crow = afr(4096, 1024, 1)
        comb_fm = afr(5120, 1024, 16)
        lsum = afr(3072, 512)
        lacc16 = afr(5120, 1024, 16)
        x_h = afr(0, 1024)
        f_h = afr(1024, 1024)
        g_h = afr(2048, 1024)
        s_h = afr(3072, 1024)

        P_A = st.enter_context(nc.psum_tensor("P_A", [128, 1024], F32))
        P_B = st.enter_context(nc.psum_tensor("P_B", [128, 1024], F32))
        P_C = st.enter_context(nc.psum_tensor("P_C", [128, 1024], F32))
        P_D = st.enter_context(nc.psum_tensor("P_D", [128, 1024], F32))

        for sname in ("dS", "dG", "cT", "cV", "cA", "cC"):
            SEM[sname] = st.enter_context(nc.semaphore(sname))
        block = st.enter_context(nc.Block())

        E("vector", "memset", ones_f[:, :], 1.0)
        E("vector", "memset", onesr_bf[:, :], 1.0)
        E("vector", "memset", ones2_bf[:, :], 1.0)
        E("vector", "memset", negc[:, :], NEGC, inc="cV")
        cv_consts = bump("cV")

        E("sync", "dma_start", ident[:, :], ident_d[:, :], inc="dS", incn=16); dS_ident = bump("dS", 16)
        E("sync", "dma_start", oneh[:, :, :], oneh_d[:, :, :], inc="dS", incn=16); dS_oneh = bump("dS", 16)
        E("sync", "dma_start", csb_c[:, :], csc_d[:, :], inc="dS", incn=16); bump("dS", 16)
        E("sync", "dma_start", csb_s[:, :], css_d[:, :], inc="dS", incn=16); dS_cs = bump("dS", 16)
        E("sync", "dma_start", mask01[:, :, :], mask_d[:, :, :], inc="dS", incn=16); dS_mask = bump("dS", 16)
        E("sync", "dma_start", biasqk_r[:, :], wqkvb_d.rearrange("(a p) -> a p", p=128),
          inc="dS", incn=16); dS_bqk = bump("dS", 16)

        E("gpsimd", "dma_start", vbias_row[:, :], wqkvb_d[2 * D:3 * D].unsqueeze(0), inc="dG", incn=16)
        dG_vb = bump("dG", 16)
        E("gpsimd", "dma_start", wa_bf[:, :, :], wa_d.rearrange("(a p) e -> p a e", p=128),
          inc="dG", incn=16); dG_wa = bump("dG", 16)
        E("gpsimd", "dma_start", rb2_bf[:, :], rb2_d[:, :], inc="dG", incn=16); bump("dG", 16)
        E("gpsimd", "dma_start", sb2_bf[:, :], sb2_d[:, :], inc="dG", incn=16); dG_b2 = bump("dG", 16)
        dG_wq = []
        for i in range(12):
            E("gpsimd", "dma_start", wq_bf[i * 512:(i + 1) * 512, :], wqkv_d[i * 512:(i + 1) * 512, :],
              inc="dG", incn=16)
            dG_wq.append(bump("dG", 16))

        E("tensor", "transpose", P_D[:, 0:48], biasqk_r[:, :], ident[0:48, 0:48],
          w=[("dS", dS_bqk), ("dS", dS_ident)], inc="cT")
        fill["PD0"] = ("cT", bump("cT"))
        E("vector", "tensor_copy", biasqk[:, :], P_D[:, 0:48], w=ge("PD0"), inc="cV")
        evac["PD0"] = ("cV", bump("cV"))
        E("vector", "tensor_scalar", biasqk[:, 16:32], biasqk[:, 16:32], SCALE, None, ALU.mult,
          w=[("cV", CNT["cV"])], inc="cV")
        cv_biasqk = bump("cV")

        xrel = [0, 0]
        xnb_rel = [0, 0]
        for tc in range(8):
            b = tc % 2
            E("sync", "dma_start", x_t[b], x_d[tc * 128:(tc + 1) * 128, :],
              w=[("cV", xrel[b])], inc="dS", incn=16)
            dS_x = bump("dS", 16)
            E("vector", "tensor_reduce", topk_s[:, 0:1], x_t[b], AX, ALU.add,
              w=[("dS", dS_x)], inc="cV")
            cV_sum = bump("cV")
            E("scalar", "activation", ln_scr, x_t[b], AF.Square, accum_out=topk_s[:, 1:2],
              w=[("dS", dS_x), ("cV", cV_sum)], inc="cA")
            cA_sq = bump("cA")
            E("vector", "tensor_scalar", topk_s[:, 2:3], topk_s[:, 0:1], 1.0 / D, None, ALU.mult,
              w=[("cA", cA_sq)], inc="cV")
            cv_e = bump("cV")
            E("vector", "tensor_scalar", topk_s[:, 3:4], topk_s[:, 1:2], 1.0 / D, None, ALU.mult,
              w=[("cV", cv_e)], inc="cV")
            cv_e = bump("cV")
            E("vector", "tensor_tensor", topk_s[:, 4:5], topk_s[:, 2:3], topk_s[:, 2:3], ALU.mult,
              w=[("cV", cv_e)], inc="cV")
            cv_e = bump("cV")
            E("vector", "tensor_tensor", topk_s[:, 3:4], topk_s[:, 3:4], topk_s[:, 4:5], ALU.subtract,
              w=[("cV", cv_e)], inc="cV")
            cv_e = bump("cV")
            E("vector", "tensor_scalar", topk_s[:, 3:4], topk_s[:, 3:4], 1.0, EPS, ALU.mult, ALU.add,
              w=[("cV", cv_e)], inc="cV")
            cV_var = bump("cV")
            E("scalar", "activation", topk_s[:, 5:6], topk_s[:, 3:4], AF.Ln, w=[("cV", cV_var)],
              inc="cA")
            ca_e = bump("cA")
            E("scalar", "activation", topk_s[:, 6:7], topk_s[:, 5:6], AF.Exp, scale=-0.5,
              w=[("cA", ca_e)], inc="cA")
            cA_rstd = bump("cA")
            E("vector", "tensor_scalar", xn_bf_t[b], x_t[b], topk_s[:, 2:3], topk_s[:, 6:7],
              ALU.subtract, ALU.mult, w=[("cA", cA_rstd), ("dS", xnb_rel[b])], inc="cV")
            cV_xnb = bump("cV")
            E("vector", "tensor_scalar", out_acc[:, tc, :], x_t[b], topk_s[:, 2:3], topk_s[:, 6:7],
              ALU.subtract, ALU.mult, inc="cV")
            xrel[b] = bump("cV")
            E("sync", "dma_start", xn_dram[tc * 128:(tc + 1) * 128, :], xn_bf_t[b],
              w=[("cV", cV_xnb)], inc="dS", incn=16)
            xnb_rel[b] = bump("dS", 16)
        dS_xnd = CNT["dS"]
        cV_ln_done = CNT["cV"]
        cA_ln_done = CNT["cA"]

        for dc in range(DC):
            E("sync", "dma_start", xn_fm[:, dc, :], xn_dram[:, dc * 128:(dc + 1) * 128],
              w=[("dS", dS_xnd)] if dc == 0 else [], transpose=True, inc="dS", incn=16)
            bump("dS", 16)
        dS_xnfm = CNT["dS"]

        wqT_loaded = [0, 0]
        wqT_free = [0, 0]

        def load_wqT(og, buf):
            for dc in range(DC):
                E("sync", "dma_start", wqT[buf][:, dc, :],
                  wq_bf[og * 512:(og + 1) * 512, dc * 128:(dc + 1) * 128],
                  w=([("dG", dG_wq[og]), ("cT", wqT_free[buf])] if dc == 0 else []),
                  transpose=True, inc="dS", incn=16)
                wqT_loaded[buf] = bump("dS", 16)

        load_wqT(0, 0)
        evrel = [0, 0, 0, 0]
        for og in range(12):
            buf = og % 2
            if og + 1 < 12:
                load_wqT(og + 1, 1 - buf)
            if og < 8:
                for oc2 in range(4):
                    oc = og * 4 + oc2
                    is_k = oc >= 16
                    wts = gf("PA0", "PA1") + [("dS", wqT_loaded[buf]), ("dS", dS_xnfm)]
                    for half in range(2):
                        for dc in range(DC):
                            E("tensor", "matmul", P_A[:, half * 512:(half + 1) * 512],
                              wqT[buf][:, dc, oc2 * 128:(oc2 + 1) * 128],
                              xn_fm[:, dc, half * 512:(half + 1) * 512],
                              w=wts, start=(dc == 0), stop=(dc == DC - 1),
                              inc="cT" if (half == 1 and dc == DC - 1) else None)
                            wts = []
                    fill["PA0"] = fill["PA1"] = ("cT", bump("cT"))
                    if oc2 == 3:
                        wqT_free[buf] = CNT["cT"]
                    ev = qkv_ev[oc % 4]
                    E("scalar", "activation", ev, P_A[:, :], AF.Identity,
                      bias=biasqk[:, oc:oc + 1], scale=SCALE if is_k else 1.0,
                      w=ge("PA0") + [("cV", cv_biasqk), ("dS", evrel[oc % 4])], inc="cA")
                    evac["PA0"] = evac["PA1"] = ("cA", bump("cA"))
                    cA_ev = CNT["cA"]
                    E("vector", "stream_shuffle", rope_scr[0:32, 0, :], ev[0:32, :],
                      list(range(16, 32)) + list(range(16)),
                      w=[("cA", cA_ev), ("dS", dS_cs)], inc="cV")
                    cv_e = bump("cV")
                    E("vector", "tensor_tensor", rope_scr[0:32, 0, :], rope_scr[0:32, 0, :],
                      csb_s[:, :], ALU.mult, w=[("cV", cv_e)], inc="cV")
                    cv_e = bump("cV")
                    E("vector", "tensor_tensor", ev[0:32, :], ev[0:32, :], csb_c[:, :], ALU.mult,
                      inc="cV")
                    cv_e = bump("cV")
                    E("vector", "tensor_tensor", ev[0:32, :], ev[0:32, :], rope_scr[0:32, 0, :],
                      ALU.add, w=[("cV", cv_e)], inc="cV")
                    cV_rope = bump("cV")
                    dst = kc_dram if is_k else q_dram
                    row0 = ((oc - 16) if is_k else oc) * 128
                    E("sync", "dma_start", dst[row0:row0 + 128, :], ev,
                      w=[("cV", cV_rope)], inc="dS", incn=16)
                    evrel[oc % 4] = bump("dS", 16)
            else:
                ogv = og - 8
                for tc in range(8):
                    pk = "PB0" if tc % 2 == 0 else "PB1"
                    half = P_B[:, 0:512] if tc % 2 == 0 else P_B[:, 512:1024]
                    wts = gf(pk) + [("dS", wqT_loaded[buf]), ("dG", dG_vb), ("cV", cv_consts)]
                    for dc in range(DC):
                        E("tensor", "matmul", half, xn_fm[:, dc, tc * 128:(tc + 1) * 128],
                          wqT[buf][:, dc, :], w=wts, start=(dc == 0), stop=False)
                        wts = []
                    E("tensor", "matmul", half, onesr_bf[0:1, :],
                      vbias_row[0:1, ogv * 512:(ogv + 1) * 512], start=False, stop=True, inc="cT")
                    fill[pk] = ("cT", bump("cT"))
                    if tc >= 6:
                        wqT_free[buf] = CNT["cT"]
                    ev = qkv_ev[tc % 2]
                    E("scalar", "activation", ev[:, 0:512], half, AF.Copy,
                      w=ge(pk) + [("dS", evrel[tc % 2])], inc="cA")
                    evac[pk] = ("cA", bump("cA"))
                    E("sync", "dma_start",
                      vc_dram[tc * 128:(tc + 1) * 128, ogv * 512:(ogv + 1) * 512], ev[:, 0:512],
                      w=[("cA", CNT["cA"])], inc="dS", incn=16)
                    evrel[tc % 2] = bump("dS", 16)
        dS_contrib = CNT["dS"]
        cT_qkv = CNT["cT"]

        E("gpsimd", "collective_compute", "AllGather", ALU.bypass,
          replica_groups=[list(range(NCORE))], ins=[kc_dram[:, :].opt()],
          outs=[kg_dram[:, :].opt()], w=[("dS", dS_contrib)], inc="cC")
        bump("cC")
        E("gpsimd", "collective_compute", "AllGather", ALU.bypass,
          replica_groups=[list(range(NCORE))], ins=[vc_dram[:, :].opt()],
          outs=[vg_dram[:, :].opt()], inc="cC")
        cC_done = bump("cC")

        E("gpsimd", "dma_start", rgb_sb, rgb_d[:, :], w=[("cV", cV_ln_done)], inc="dG", incn=16)
        bump("dG", 16)
        E("gpsimd", "dma_start", rb1_sb, rb1_d[:, :], inc="dG", incn=16); bump("dG", 16)
        E("gpsimd", "dma_start", sgb_sb, sgb_d[:, :], inc="dG", incn=16); bump("dG", 16)
        E("gpsimd", "dma_start", sb1_sb, sb1_d[:, :], inc="dG", incn=16)
        dG_bias = bump("dG", 16)

        for (src_t, dstc, n) in ((rgb_sb, bcol_g, 16), (rb1_sb, bcol_w, 16),
                                 (sgb_sb, bcol_gs, 2), (sb1_sb, bcol_ws, 2)):
            for fc in range(FC):
                E("tensor", "transpose", P_D[:, 0:n], src_t[0:n, fc * 128:(fc + 1) * 128],
                  ident[0:n, 0:n], w=gf("PD0") + [("dG", dG_bias)], inc="cT")
                fill["PD0"] = ("cT", bump("cT"))
                E("vector", "tensor_copy", dstc[:, fc, 0:n], P_D[:, 0:n], w=ge("PD0"), inc="cV")
                evac["PD0"] = ("cV", bump("cV"))
        cT_biasT = CNT["cT"]
        cV_biasT = CNT["cV"]

        wts = gf("PD0", "PD1") + [("dG", dG_wa), ("dS", dS_xnfm)]
        for half in range(2):
            for dc in range(DC):
                E("tensor", "matmul", P_D[0:16, half * 512:(half + 1) * 512], wa_bf[:, dc, :],
                  xn_fm[:, dc, half * 512:(half + 1) * 512], w=wts, start=(dc == 0),
                  stop=(dc == DC - 1), inc="cT" if (half == 1 and dc == DC - 1) else None)
                wts = []
        fill["PD0"] = fill["PD1"] = ("cT", bump("cT"))
        E("scalar", "activation", aff_fm, P_D[0:16, :], AF.Sigmoid,
          w=ge("PD0") + [("cA", cA_ln_done), ("cV", cV_ln_done)], inc="cA")
        evac["PD0"] = evac["PD1"] = ("cA", bump("cA"))
        cA_aff = CNT["cA"]

        for tc in range(8):
            E("tensor", "transpose", P_D[:, 0:16], aff_fm[0:16, tc * 128:(tc + 1) * 128],
              ident[0:16, 0:16], w=gf("PD0") + [("cA", cA_aff)], inc="cT")
            fill["PD0"] = ("cT", bump("cT"))
            E("vector", "tensor_copy", aff_tm[:, tc, :], P_D[:, 0:16], w=ge("PD0"), inc="cV")
            evac["PD0"] = ("cV", bump("cV"))
            a = aff_tm[:, tc, :]
            E("vector", "tensor_reduce", topk_s[:, 0:1], a, AX, ALU.max, w=[("cV", CNT["cV"])],
              inc="cV")
            cv_e = bump("cV")
            E("vector", "tensor_single_scalar", comb_tm[:, tc, :], a, topk_s[:, 0:1], ALU.is_ge,
              w=[("cV", cv_e)], inc="cV")
            cv_e = bump("cV")
            E("vector", "scalar_tensor_tensor", comb_tm[:, tc, :], comb_tm[:, tc, :], -1e9, a,
              ALU.mult, ALU.add, w=[("cV", cv_e)], inc="cV")
            cv_e = bump("cV")
            E("vector", "tensor_reduce", topk_s[:, 1:2], comb_tm[:, tc, :], AX, ALU.max,
              w=[("cV", cv_e)], inc="cV")
            cv_e = bump("cV")
            E("vector", "tensor_tensor", topk_s[:, 2:3], topk_s[:, 0:1], topk_s[:, 1:2], ALU.add,
              w=[("cV", cv_e)], inc="cV")
            cV_den = bump("cV")
            E("scalar", "activation", topk_s[:, 3:4], topk_s[:, 2:3], AF.Ln, w=[("cV", cV_den)],
              inc="cA")
            ca_e = bump("cA")
            E("scalar", "activation", topk_s[:, 4:5], topk_s[:, 3:4], AF.Exp, scale=-1.0,
              w=[("cA", ca_e)], inc="cA")
            cA_rd = bump("cA")
            E("vector", "tensor_single_scalar", comb_tm[:, tc, :], a, topk_s[:, 1:2], ALU.is_ge,
              w=[("cA", cA_rd)], inc="cV")
            cv_e = bump("cV")
            E("vector", "tensor_tensor", comb_tm[:, tc, :], comb_tm[:, tc, :], a, ALU.mult,
              w=[("cV", cv_e)], inc="cV")
            cv_e = bump("cV")
            E("vector", "tensor_scalar", comb_tm[:, tc, :], comb_tm[:, tc, :], topk_s[:, 4:5], None,
              ALU.mult, w=[("cV", cv_e)], inc="cV")
            cV_ctm = bump("cV")
            E("tensor", "transpose", P_D[0:16, 512:640], comb_tm[:, tc, :], ident[:, :],
              w=gf("PD1") + [("cV", cV_ctm)], inc="cT")
            fill["PD1"] = ("cT", bump("cT"))
            E("vector", "tensor_copy", comb_fm[0:16, tc * 128:(tc + 1) * 128], P_D[0:16, 512:640],
              w=ge("PD1"), inc="cV")
            evac["PD1"] = ("cV", bump("cV"))
        E("vector", "tensor_copy", comb_bf[:, :], comb_fm, inc="cV")
        cV_comb = bump("cV")
        crow_guard = CNT["cT"]

        def wsrc(u):
            if u < NRE:
                return rg_d[u], rw1_d[u], rw2_d[u], bcol_g, bcol_w, u, True
            s = u - NRE
            return sg_d[s], sw1_d[s], sw2_d[s], bcol_gs, bcol_ws, s, False

        gw_loaded = [0, 0]
        gw_free = [0, 0]
        w2_loaded = 0
        w2_free = 0

        def load_gw(u, fc, buf, extra=()):
            nonlocal gw_loaded
            g_src, w1_src = wsrc(u)[0], wsrc(u)[1]
            first = True
            for (srcw, dstb) in ((g_src, gwg[buf]), (w1_src, gww[buf])):
                v = srcw.rearrange("(a p) f -> p a f", p=128)[:, :, fc * 128:(fc + 1) * 128]
                E("gpsimd", "dma_start", dstb[:, 0:8, :], v[:, 0:8, :],
                  w=([("cT", gw_free[buf])] + list(extra)) if first else [], inc="dG", incn=16)
                first = False
                bump("dG", 16)
                E("gpsimd", "dma_start", dstb[:, 8:16, :], v[:, 8:16, :], inc="dG", incn=16)
                gw_loaded[buf] = bump("dG", 16)

        def load_w2(u, extra=()):
            nonlocal w2_loaded
            w2_src = wsrc(u)[2]
            for fc in range(FC):
                E("gpsimd", "dma_start", w2buf[:, fc, :], w2_src[fc * 128:(fc + 1) * 128, :],
                  w=([("cT", w2_free)] + list(extra)) if fc == 0 else [], inc="dG", incn=16)
                w2_loaded = bump("dG", 16)

        load_gw(0, 0, 0, extra=[("cT", cT_qkv)])
        load_w2(0, extra=[("dS", dS_contrib)])

        cV_tgw = 0
        cV_cbc_user = 0
        dG_crow = 0
        for u in range(NU):
            _, _, _, bg, bw, bi, routed = wsrc(u)
            if routed:
                E("gpsimd", "dma_start", crow[0:1, :], comb_fm[bi:bi + 1, :],
                  w=[("cV", cV_comb), ("cT", crow_guard)], inc="dG", incn=16)
                dG_crow = bump("dG", 16)
                E("tensor", "matmul", P_D[:, 0:512], ones_f[0:1, :], crow[0:1, 0:512],
                  w=gf("PD0", "PD1") + [("dG", dG_crow)], start=True, stop=True)
                E("tensor", "matmul", P_D[:, 512:1024], ones_f[0:1, :], crow[0:1, 512:1024],
                  start=True, stop=True, inc="cT")
                fill["PD0"] = fill["PD1"] = ("cT", bump("cT"))
                crow_guard = CNT["cT"]
                E("vector", "tensor_copy", c_bc, P_D[:, :],
                  w=ge("PD0") + [("cV", cV_cbc_user), ("cV", cV_biasT)], inc="cV")
                evac["PD0"] = evac["PD1"] = ("cV", bump("cV"))
            for fc in range(FC):
                buf = (u * FC + fc) % 2
                nfc, nu_ = fc + 1, u
                if nfc == FC:
                    nfc, nu_ = 0, u + 1
                if nu_ < NU:
                    load_gw(nu_, nfc, 1 - buf)
                wts = gf("PA0", "PA1") + [("dG", gw_loaded[buf]), ("dS", dS_xnfm)]
                for half in range(2):
                    for dc in range(DC):
                        E("tensor", "matmul", P_A[:, half * 512:(half + 1) * 512], gwg[buf][:, dc, :],
                          xn_fm[:, dc, half * 512:(half + 1) * 512], w=wts, start=(dc == 0),
                          stop=(dc == DC - 1), inc="cT" if (half == 1 and dc == DC - 1) else None)
                        wts = []
                fill["PA0"] = fill["PA1"] = ("cT", bump("cT"))
                wts = gf("PB0", "PB1")
                for half in range(2):
                    for dc in range(DC):
                        E("tensor", "matmul", P_B[:, half * 512:(half + 1) * 512], gww[buf][:, dc, :],
                          xn_fm[:, dc, half * 512:(half + 1) * 512], w=wts, start=(dc == 0),
                          stop=(dc == DC - 1), inc="cT" if (half == 1 and dc == DC - 1) else None)
                        wts = []
                fill["PB0"] = fill["PB1"] = ("cT", bump("cT"))
                gw_free[buf] = CNT["cT"]
                E("scalar", "activation", t_g, P_A[:, :], AF.Gelu, bias=bg[:, fc, bi:bi + 1],
                  w=ge("PA0") + [("cV", max(cV_tgw, cV_biasT))], inc="cA")
                evac["PA0"] = evac["PA1"] = ("cA", bump("cA"))
                E("scalar", "activation", t_w, P_B[:, :], AF.Identity, bias=bw[:, fc, bi:bi + 1],
                  w=ge("PB0"), inc="cA")
                evac["PB0"] = evac["PB1"] = ("cA", bump("cA"))
                cA_tgw = CNT["cA"]
                if routed:
                    E("vector", "tensor_tensor", t_g, t_g, t_w, ALU.mult, w=[("cA", cA_tgw)])
                    E("vector", "tensor_tensor", y_sb[:, fc, :], t_g, c_bc, ALU.mult, inc="cV")
                else:
                    E("vector", "tensor_tensor", y_sb[:, fc, :], t_g, t_w, ALU.mult,
                      w=[("cA", cA_tgw)], inc="cV")
                cV_tgw = bump("cV")
            cV_yu = CNT["cV"]
            cV_cbc_user = cV_yu
            last = (u == NU - 1)
            for tc in range(8):
                for win in range(4):
                    pk = "PC0" if (tc * 4 + win) % 2 == 0 else "PC1"
                    half = P_C[:, 0:512] if pk == "PC0" else P_C[:, 512:1024]
                    wts = gf(pk) + [("dG", w2_loaded), ("cV", cV_yu)]
                    for fc in range(FC):
                        stop = (fc == FC - 1) and not last
                        E("tensor", "matmul", half, y_sb[:, fc, tc * 128:(tc + 1) * 128],
                          w2buf[:, fc, win * 512:(win + 1) * 512], w=wts, start=(fc == 0),
                          stop=stop, inc="cT" if stop else None)
                        wts = []
                    if last:
                        E("tensor", "matmul", half, comb_bf[0:16, tc * 128:(tc + 1) * 128],
                          rb2_bf[0:16, win * 512:(win + 1) * 512],
                          w=[("dG", dG_b2), ("cV", cV_comb)], start=False, stop=False)
                        E("tensor", "matmul", half, ones2_bf[0:2, :],
                          sb2_bf[0:2, win * 512:(win + 1) * 512], start=False, stop=True, inc="cT")
                    fill[pk] = ("cT", bump("cT"))
                    E("vector", "tensor_tensor", out_acc[:, tc, win * 512:(win + 1) * 512], half,
                      out_acc[:, tc, win * 512:(win + 1) * 512], ALU.add, w=ge(pk), inc="cV")
                    evac[pk] = ("cV", bump("cV"))
            w2_free = CNT["cT"]
            if u + 1 < NU:
                load_w2(u + 1)
        cV_ffn = CNT["cV"]
        cT_ffn = CNT["cT"]

        for tc in range(8):
            E("sync", "dma_start", ffn_dram[tc * 128:(tc + 1) * 128, :], out_acc[:, tc, :],
              w=[("cV", cV_ffn)] if tc == 0 else [], inc="dS", incn=16)
            bump("dS", 16)
        dS_park = CNT["dS"]

        E("sync", "dma_start", q_sb[:, :, :], q_dram.rearrange("(h p) t -> p h t", p=128),
          w=[("cT", cT_ffn)], inc="dS", incn=16)
        dS_qsb = bump("dS", 16)

        kg_v = kg_dram.rearrange("(r q) t -> r q t", r=NCORE)
        vg_v = vg_dram.rearrange("(r p2) d -> r p2 d", r=NCORE)
        kv_loaded = [0, 0]
        kv_free = [cT_ffn, cT_ffn]
        pt_free = [cT_ffn, cT_ffn]
        pt_vfree = [0, 0]
        lsum_mm = cT_ffn
        lacc_guard = dG_crow
        sacc_first = True

        def load_kv(w_, h_, b_):
            nonlocal kv_loaded
            E("sync", "dma_start", k_h[b_][:, :, :],
              kg_v[:, h_ * 128:(h_ + 1) * 128, w_ * 128:(w_ + 1) * 128].transpose([1, 0, 2]),
              w=[("cC", cC_done), ("cT", kv_free[b_])], inc="dS", incn=16)
            bump("dS", 16)
            E("sync", "dma_start", v_h[b_][:, :, :],
              vg_v[:, w_ * 128:(w_ + 1) * 128, h_ * 128:(h_ + 1) * 128].transpose([1, 0, 2]),
              inc="dS", incn=16)
            kv_loaded[b_] = bump("dS", 16)

        load_kv(0, 0, 0)
        for w_ in range(8):
            for h in range(H):
                b = (w_ * H + h) % 2
                if (w_, h) != (7, H - 1):
                    nw, nh = (w_, h + 1) if h + 1 < H else (w_ + 1, 0)
                    load_kv(nw, nh, 1 - b)
                for (pi, (c0, wp)) in enumerate(pieces_of(w_)):
                    pb = pi % 2
                    for r in range(NCORE):
                        sk = "PA0" if r % 2 == 0 else "PA1"
                        half = P_A[:, 0:wp] if r % 2 == 0 else P_A[:, 512:512 + wp]
                        wts = gf(sk) + ([("dS", kv_loaded[b]), ("dS", dS_qsb)] if r == 0 else [])
                        E("tensor", "matmul", half, k_h[b][:, r, :], q_sb[:, h, c0:c0 + wp],
                          w=wts, start=True, stop=True, inc="cT")
                        fill[sk] = ("cT", bump("cT"))
                        E("scalar", "activation", p_t[pb][:, r, 0:wp], half, AF.Exp,
                          bias=negc[:, 0:1],
                          w=ge(sk) + ([("cV", max(cv_consts, pt_vfree[pb])),
                                       ("cT", pt_free[pb])] if r == 0 else []),
                          inc="cA")
                        evac[sk] = ("cA", bump("cA"))
                    cA_exp = CNT["cA"]
                    cV_mask = 0
                    if c0 == w_ * 128:
                        for r in range(NCORE):
                            E("vector", "tensor_tensor", p_t[pb][:, r, 0:128], p_t[pb][:, r, 0:128],
                              mask01[:, r, :], ALU.mult,
                              w=([("cA", cA_exp), ("dS", dS_mask)] if r == 0 else []),
                              inc="cV" if r == NCORE - 1 else None)
                        cV_mask = bump("cV")
                    E("vector", "tensor_tensor", lsum[:, 0:wp], p_t[pb][:, 0, 0:wp],
                      p_t[pb][:, 1, 0:wp], ALU.add,
                      w=[("cA", cA_exp), ("cV", cV_mask), ("cT", lsum_mm)])
                    for r in range(2, NCORE):
                        E("vector", "tensor_tensor", lsum[:, 0:wp], lsum[:, 0:wp],
                          p_t[pb][:, r, 0:wp], ALU.add)
                    E("vector", "tensor_copy", lsum_bf[:, 0:wp], lsum[:, 0:wp], inc="cV")
                    cV_lsum = bump("cV")
                    pt_vfree[pb] = cV_lsum
                    E("tensor", "matmul", P_D[0:16, 0:wp], oneh[:, h, :], lsum_bf[:, 0:wp],
                      w=gf("PD0") + [("cV", cV_lsum), ("dS", dS_oneh)], start=True, stop=True,
                      inc="cT")
                    fill["PD0"] = ("cT", bump("cT"))
                    lsum_mm = CNT["cT"]
                    if w_ == 0 and h == 0:
                        E("vector", "tensor_copy", lacc16[:, c0:c0 + wp], P_D[0:16, 0:wp],
                          w=ge("PD0") + [("dG", lacc_guard)], inc="cV")
                    else:
                        E("vector", "tensor_tensor", lacc16[:, c0:c0 + wp], P_D[0:16, 0:wp],
                          lacc16[:, c0:c0 + wp], ALU.add, w=ge("PD0"), inc="cV")
                    evac["PD0"] = ("cV", bump("cV"))
                    pk = "PC0" if pb == 0 else "PC1"
                    halfc = P_C[:, 0:wp] if pb == 0 else P_C[:, 512:512 + wp]
                    wts = gf(pk) + [("cA", cA_exp), ("cV", cV_mask)]
                    for r in range(NCORE):
                        E("tensor", "matmul", halfc, v_h[b][:, r, :], p_t[pb][:, r, 0:wp],
                          w=wts, start=(r == 0), stop=(r == NCORE - 1),
                          inc="cT" if r == NCORE - 1 else None)
                        wts = []
                    fill[pk] = ("cT", bump("cT"))
                    pt_free[pb] = CNT["cT"]
                    if w_ == 0:
                        E("vector", "tensor_copy", s_acc[:, h, c0:c0 + wp], halfc,
                          w=ge(pk) + ([("dS", dS_park)] if sacc_first else []), inc="cV")
                        sacc_first = False
                    else:
                        E("vector", "tensor_tensor", s_acc[:, h, c0:c0 + wp], halfc,
                          s_acc[:, h, c0:c0 + wp], ALU.add, w=ge(pk), inc="cV")
                    evac[pk] = ("cV", bump("cV"))
                kv_free[b] = CNT["cT"]
        cV_att = CNT["cV"]

        E("scalar", "activation", lacc16, lacc16, AF.Ln, w=[("cV", cV_att)], inc="cA")
        ca_e = bump("cA")
        E("scalar", "activation", lacc16, lacc16, AF.Exp, scale=-1.0, w=[("cA", ca_e)], inc="cA")
        cA_rl = bump("cA")
        crow_free = crow_guard
        for h in range(H):
            E("gpsimd", "dma_start", crow[0:1, :], lacc16[h:h + 1, :],
              w=[("cA", cA_rl), ("cT", crow_free)], inc="dG", incn=16)
            dG_lrow = bump("dG", 16)
            E("tensor", "matmul", P_D[:, 0:512], ones_f[0:1, :], crow[0:1, 0:512],
              w=gf("PD0", "PD1") + [("dG", dG_lrow)], start=True, stop=True)
            E("tensor", "matmul", P_D[:, 512:1024], ones_f[0:1, :], crow[0:1, 512:1024],
              start=True, stop=True, inc="cT")
            fill["PD0"] = fill["PD1"] = ("cT", bump("cT"))
            crow_free = CNT["cT"]
            E("vector", "tensor_tensor", s_acc[:, h, :], P_D[:, :], s_acc[:, h, :], ALU.mult,
              w=ge("PD0"), inc="cV")
            evac["PD0"] = evac["PD1"] = ("cV", bump("cV"))
        cV_norm = CNT["cV"]

        xh_rel = ("cV", cV_ffn)
        fh_rel = ("cV", cV_ffn)
        sh_rel = ("cV", 0)
        gh_rel = ("cV", cV_ffn)
        for tc in range(8):
            for dh in range(2):
                for grp in range(2):
                    pk = "PA0" if grp == 0 else "PA1"
                    half = P_A[:, 0:512] if grp == 0 else P_A[:, 512:1024]
                    wts = gf(pk) + [("cV", cV_norm)]
                    for j in range(4):
                        h = dh * 8 + grp * 4 + j
                        E("tensor", "transpose", half[:, j * 128:(j + 1) * 128],
                          s_acc[:, h, tc * 128:(tc + 1) * 128], ident[:, :],
                          w=wts, inc="cT" if j == 3 else None)
                        wts = []
                    fill[pk] = ("cT", bump("cT"))
                    E("vector", "tensor_copy", s_h[:, grp * 512:(grp + 1) * 512], half,
                      w=ge(pk) + ([sh_rel] if grp == 0 else []), inc="cV")
                    evac[pk] = ("cV", bump("cV"))
                cV_sh = CNT["cV"]
                E("sync", "dma_start", x_h, x_d[tc * 128:(tc + 1) * 128, dh * 1024:(dh + 1) * 1024],
                  w=[xh_rel], inc="dS", incn=16)
                dS_xh = bump("dS", 16)
                E("sync", "dma_start", f_h, ffn_dram[tc * 128:(tc + 1) * 128, dh * 1024:(dh + 1) * 1024],
                  w=[("dS", dS_park), fh_rel], inc="dS", incn=16)
                dS_fh = bump("dS", 16)
                E("scalar", "activation", g_h, f_h, AF.Silu,
                  w=[("dS", dS_fh), gh_rel], inc="cA")
                cA_g = bump("cA")
                fh_rel = ("cA", cA_g)
                E("vector", "tensor_tensor", g_h, g_h, s_h, ALU.mult,
                  w=[("cA", cA_g), ("cV", cV_sh)], inc="cV")
                cV_gm = bump("cV")
                sh_rel = ("cV", cV_gm)
                E("vector", "tensor_tensor", x_h, x_h, g_h, ALU.add,
                  w=[("dS", dS_xh)], inc="cV")
                cV_out = bump("cV")
                gh_rel = ("cV", cV_out)
                E("sync", "dma_start", y_d[tc * 128:(tc + 1) * 128, dh * 1024:(dh + 1) * 1024],
                  x_h, w=[("cV", cV_out)], inc="dS", incn=16)
                xh_rel = ("dS", bump("dS", 16))
        if debug:
            taps = [("dbg_xn", xn_dram, None), ("dbg_q", q_dram, None), ("dbg_kc", kc_dram, None),
                    ("dbg_vc", vc_dram, None), ("dbg_ffn", ffn_dram, None)]
            for (nm, srcd, _) in taps:
                td = nc.dram_tensor(nm, list(srcd.tensor.shape), srcd.dtype, kind="ExternalOutput").ap()
                E("sync", "dma_start", td[:, :], srcd[:, :], w=[("dS", CNT["dS"])], inc="dS", incn=16)
                bump("dS", 16)
            for (nm, ssrc, shape) in (("dbg_comb", comb_fm, [16, 1024]),
                                      ("dbg_afftm", aff_tm[:, :, :].rearrange("p a c -> p (a c)"), [128, 128]),
                                      ("dbg_combtm", comb_tm[:, :, :].rearrange("p a c -> p (a c)"), [128, 128]),
                                      ("dbg_l", lacc16, [16, 1024]),
                                      ("dbg_sacc", B32[:, :], [128, 16384])):
                td = nc.dram_tensor(nm, shape, F32, kind="ExternalOutput").ap()
                E("sync", "dma_start", td[:, :], ssrc, w=[("dS", CNT["dS"]), ("cV", CNT["cV"])],
                  inc="dS", incn=16)
                bump("dS", 16)
        dS_final = CNT["dS"]
        ST["sync"].append(lambda e, v=dS_final: e.wait_ge(SEM["dS"], v))

        @block.tensor
        def _(te):
            for f in ST["tensor"]:
                f(te)

        @block.vector
        def _(ve):
            for f in ST["vector"]:
                f(ve)

        @block.scalar
        def _(se):
            for f in ST["scalar"]:
                f(se)

        @block.sync
        def _(sy):
            for f in ST["sync"]:
                f(sy)

        @block.gpsimd
        def _(gp):
            for f in ST["gpsimd"]:
                f(gp)

    return nc


def _host_consts(c):
    tau = np.arange(T)
    pos = (8 * (tau // 128) + c) * 128 + (tau % 128)
    inv_freq = (10000.0 ** (-(np.arange(0, 32, 2, dtype=np.float32) / 32.0))).astype(np.float32)
    freqs = pos[:, None].astype(np.float32) * inv_freq[None, :]
    cosv = np.cos(freqs).T.astype(np.float32)
    sinv = np.sin(freqs).T.astype(np.float32)
    cs_c = np.concatenate([cosv, cosv], axis=0).astype(ml_dtypes.bfloat16)
    cs_s = np.concatenate([-sinv, sinv], axis=0).astype(ml_dtypes.bfloat16)
    ik = np.arange(128)[:, None]
    iq = np.arange(128)[None, :]
    m = np.zeros((128, 8, 128), np.float32)
    for r in range(8):
        if r < c:
            m[:, r, :] = 1.0
        elif r == c:
            m[:, r, :] = (ik <= iq).astype(np.float32)
    mask01 = m.astype(ml_dtypes.bfloat16)
    ident = np.eye(128, dtype=np.float32)
    return cs_c, cs_s, mask01, ident


def kernel(L0_attn_x, L0_attn_Wqkv_w, L0_attn_Wqkv_b, L0_ffn_wa, L0_ffn_rg, L0_ffn_rgb,
           L0_ffn_rw1, L0_ffn_rb1, L0_ffn_rw2, L0_ffn_rb2, L0_ffn_sg, L0_ffn_sgb,
           L0_ffn_sw1, L0_ffn_sb1, L0_ffn_sw2, L0_ffn_sb2):
    f32 = np.float32
    x = np.ascontiguousarray(np.asarray(L0_attn_x, f32).reshape(8192, D))
    shared = {
        "wqkv": np.ascontiguousarray(np.asarray(L0_attn_Wqkv_w, f32)),
        "wqkv_b": np.ascontiguousarray(np.asarray(L0_attn_Wqkv_b, f32)),
        "wa": np.ascontiguousarray(np.asarray(L0_ffn_wa, f32)),
        "rg": np.ascontiguousarray(np.asarray(L0_ffn_rg, f32)),
        "rgb": np.ascontiguousarray(np.asarray(L0_ffn_rgb, f32)),
        "rw1": np.ascontiguousarray(np.asarray(L0_ffn_rw1, f32)),
        "rb1": np.ascontiguousarray(np.asarray(L0_ffn_rb1, f32)),
        "rw2": np.ascontiguousarray(np.asarray(L0_ffn_rw2, f32)),
        "rb2": np.ascontiguousarray(np.asarray(L0_ffn_rb2, f32)),
        "sg": np.ascontiguousarray(np.asarray(L0_ffn_sg, f32)),
        "sgb": np.ascontiguousarray(np.asarray(L0_ffn_sgb, f32)),
        "sw1": np.ascontiguousarray(np.asarray(L0_ffn_sw1, f32)),
        "sb1": np.ascontiguousarray(np.asarray(L0_ffn_sb1, f32)),
        "sw2": np.ascontiguousarray(np.asarray(L0_ffn_sw2, f32)),
        "sb2": np.ascontiguousarray(np.asarray(L0_ffn_sb2, f32)),
    }
    x_tiles = x.reshape(64, 128, D)
    in_maps = []
    for c in range(NCORE):
        cs_c, cs_s, mask01, ident = _host_consts(c)
        m = dict(shared)
        m["x_sh"] = np.ascontiguousarray(x_tiles[c::8].reshape(T, D))
        m["cs_c"] = cs_c
        m["cs_s"] = cs_s
        m["mask01"] = mask01
        m["ident"] = ident
        m["oneh16"] = np.broadcast_to(np.eye(16, dtype=ml_dtypes.bfloat16)[None, :, :],
                                      (128, 16, 16)).copy()
        in_maps.append(m)

    nc = build_nc()
    res = run_bass_kernel_spmd(nc, in_maps, core_ids=list(range(NCORE)))

    out = np.empty((8192, D), f32)
    ot = out.reshape(64, 128, D)
    for c in range(NCORE):
        ot[c::8] = np.asarray(res.results[c]["y"]).reshape(8, 128, D)
    return out.reshape(1, 8192, D)


# version 06
# speedup vs baseline: 1.0773x; 1.0773x over previous
import numpy as np
import ml_dtypes
import concourse.bass as bass
import concourse.mybir as mybir
from concourse.bass_utils import run_bass_kernel_spmd
from contextlib import ExitStack

F32 = mybir.dt.float32
BF16 = mybir.dt.bfloat16
AF = mybir.ActivationFunctionType
ALU = mybir.AluOpType
AX = mybir.AxisListType.X

NCORE = 8
D = 2048
H = 16
FF = 1024
NRE = 16
NU = 18
T = 1024
DC = 16
FC = 8
SCALE = 1.0 / float(np.sqrt(128.0))
NEGC = -12.0
EPS = 1e-5


def pieces_of(w):
    c0 = w * 128
    W = 1024 - c0
    out = [(c0, min(512, W))]
    if W > 512:
        out.append((c0 + 512, W - 512))
    return out


def build_nc(debug=False):
    nc = bass.Bass("TRN2", target_bir_lowering=False, debug=False)

    def din(name, shape, dt=F32):
        return nc.dram_tensor(name, shape, dt, kind="ExternalInput").ap()

    x_d = din("x_sh", [T, D])
    wqkv_d = din("wqkv", [3 * D, D])
    wqkvb_d = din("wqkv_b", [3 * D])
    wa_d = din("wa", [D, 16])
    rg_d = din("rg", [NRE, D, FF]); rw1_d = din("rw1", [NRE, D, FF]); rw2_d = din("rw2", [NRE, FF, D])
    rgb_d = din("rgb", [NRE, FF]); rb1_d = din("rb1", [NRE, FF]); rb2_d = din("rb2", [NRE, D])
    sg_d = din("sg", [2, D, FF]); sw1_d = din("sw1", [2, D, FF]); sw2_d = din("sw2", [2, FF, D])
    sgb_d = din("sgb", [2, FF]); sb1_d = din("sb1", [2, FF]); sb2_d = din("sb2", [2, D])
    csc_d = din("cs_c", [32, T], BF16)
    css_d = din("cs_s", [32, T], BF16)
    mask_d = din("mask01", [128, 8, 128], BF16)
    ident_d = din("ident", [128, 128])
    oneh_d = din("oneh16", [128, H, 16], BF16)
    y_d = nc.dram_tensor("y", [T, D], F32, kind="ExternalOutput").ap()

    xn_dram = nc.dram_tensor("xn_dram", [T, D], BF16, kind="Internal").ap()
    wq_bf = nc.dram_tensor("wq_bf", [3 * D, D], BF16, kind="Internal").ap()
    q_dram = nc.dram_tensor("q_dram", [D, T], BF16, kind="Internal").ap()
    kc_dram = nc.dram_tensor("kc_dram", [D, T], BF16, kind="Internal").ap()
    vc_dram = nc.dram_tensor("vc_dram", [T, D], BF16, kind="Internal").ap()
    kg_dram = nc.dram_tensor("kg_dram", [NCORE * D, T], BF16, kind="Internal").ap()
    vg_dram = nc.dram_tensor("vg_dram", [NCORE * T, D], BF16, kind="Internal").ap()
    ffn_dram = nc.dram_tensor("ffn_dram", [T, D], F32, kind="Internal").ap()

    ST = {e: [] for e in ("tensor", "vector", "scalar", "sync", "gpsimd")}
    CNT = {"dS": 0, "dG": 0, "cT": 0, "cV": 0, "cA": 0, "cC": 0}
    SEM = {}

    def E(eng, method, *a, w=(), inc=None, incn=1, **kw):
        wm = {}
        for (sn, v) in w:
            if v > 0 and v > wm.get(sn, 0):
                wm[sn] = v
        wl = list(wm.items())

        def f(e, method=method, a=a, kw=kw, wl=wl, inc=inc, incn=incn):
            for (sn, v) in wl:
                e.wait_ge(SEM[sn], v)
            r = getattr(e, method)(*a, **kw)
            if inc:
                r.then_inc(SEM[inc], incn)
        ST[eng].append(f)

    def bump(sn, n=1):
        CNT[sn] += n
        return CNT[sn]

    fill = {}
    evac = {}

    def gf(*keys):
        return [evac[k] for k in keys if k in evac]

    def ge(*keys):
        return [fill[k] for k in keys if k in fill]

    with ExitStack() as st:
        def sb(name, shape, dt):
            return st.enter_context(nc.sbuf_tensor(name, shape, dt))

        ident = sb("identW", [128, 128], F32)
        csb_c = sb("csb_c", [32, T], BF16)
        csb_s = sb("csb_s", [32, T], BF16)
        mask01 = sb("mask01b", [128, 8, 128], BF16)
        biasqk = sb("biasqk", [128, 48], F32)
        biasqk_r = sb("biasqk_r", [48, 128], F32)
        oneh = sb("oneh", [128, H, 16], BF16)
        ones_f = sb("ones_f", [1, 128], F32)
        onesr_bf = sb("onesr_bf", [1, 128], BF16)
        ones2_bf = sb("ones2_bf", [2, 128], BF16)
        negc = sb("negc", [128, 1], F32)
        bcol_g = sb("bcol_g", [128, FC, NRE], F32)
        bcol_w = sb("bcol_w", [128, FC, NRE], F32)
        bcol_gs = sb("bcol_gs", [128, FC, 2], F32)
        bcol_ws = sb("bcol_ws", [128, FC, 2], F32)
        rb2_bf = sb("rb2_bf", [16, D], BF16)
        sb2_bf = sb("sb2_bf", [2, D], BF16)
        wa_bf = sb("wa_bf", [128, DC, 16], BF16)
        aff_tm = sb("aff_tm", [128, 8, 16], F32)
        comb_tm = sb("comb_tm", [128, 8, 16], F32)
        comb_bf = sb("comb_bf", [16, T], BF16)
        topk_s = sb("topk_s", [128, 8], F32)

        B32 = sb("B32", [128, 16384], F32)
        ABF = sb("ABF", [128, 48 * 1024], BF16)
        AFR = sb("AFR", [128, 6144], F32)

        out_acc = B32[:, :].rearrange("p (a c) -> p a c", a=8)
        s_acc = B32[:, :].rearrange("p (a c) -> p a c", a=16)

        def abf(off, shape):
            n = int(np.prod(shape))
            ap = ABF[:, off:off + n]
            if len(shape) == 1:
                return ap
            return ap.rearrange("p (a c) -> p a c", a=shape[0])

        xn_fm = abf(0, [16, 1024])
        wqT = [abf(16384, [16, 512]), abf(24576, [16, 512])]
        qkv_ev = [abf(32768 + i * 1024, [1024]) for i in range(4)]
        xn_bf_t = [abf(36864, [2048]), abf(38912, [2048])]
        rope_scr = abf(40960, [3, 1024])
        vbias_row = ABF[0:1, 41984:44032]

        y_sb = abf(16384, [8, 1024])
        gwg = [abf(24576, [16, 128]), abf(26624, [16, 128])]
        gww = [abf(28672, [16, 128]), abf(30720, [16, 128])]
        w2buf = abf(32768, [8, 2048])

        q_sb = abf(0, [16, 1024])
        k_h = [abf(16384, [8, 128]), abf(17408, [8, 128])]
        v_h = [abf(18432, [8, 128]), abf(19456, [8, 128])]
        p_t = [abf(20480, [8, 512]), abf(24576, [8, 512])]
        lsum_bf = abf(28672, [512])

        def afr(off, n, p=128):
            return AFR[0:p, off:off + n]

        x_t = [afr(0, 2048), afr(2048, 2048)]
        ln_scr = afr(4096, 2048)
        rgb_sb = afr(0, 1024, 16)
        rb1_sb = afr(1024, 1024, 16)
        sgb_sb = afr(2048, 1024, 2)
        sb1_sb = afr(3072, 1024, 2)
        aff_fm = afr(4096, 1024, 16)
        t_g = afr(0, 1024)
        t_w = afr(1024, 1024)
        c_bc = afr(2048, 1024)
        crow = afr(4096, 1024, 1)
        comb_fm = afr(5120, 1024, 16)
        lsum = afr(3072, 512)
        lacc16 = afr(5120, 1024, 16)
        x_h = afr(0, 1024)
        f_h = afr(1024, 1024)
        g_h = afr(2048, 1024)
        s_h = afr(3072, 1024)

        P_A = st.enter_context(nc.psum_tensor("P_A", [128, 1024], F32))
        P_B = st.enter_context(nc.psum_tensor("P_B", [128, 1024], F32))
        P_C = st.enter_context(nc.psum_tensor("P_C", [128, 1024], F32))
        P_D = st.enter_context(nc.psum_tensor("P_D", [128, 1024], F32))

        for sname in ("dS", "dG", "cT", "cV", "cA", "cC"):
            SEM[sname] = st.enter_context(nc.semaphore(sname))
        block = st.enter_context(nc.Block())

        E("vector", "memset", ones_f[:, :], 1.0)
        E("vector", "memset", onesr_bf[:, :], 1.0)
        E("vector", "memset", ones2_bf[:, :], 1.0)
        E("vector", "memset", negc[:, :], NEGC, inc="cV")
        cv_consts = bump("cV")

        E("sync", "dma_start", ident[:, :], ident_d[:, :], inc="dS", incn=16); dS_ident = bump("dS", 16)
        E("sync", "dma_start", oneh[:, :, :], oneh_d[:, :, :], inc="dS", incn=16); dS_oneh = bump("dS", 16)
        E("sync", "dma_start", csb_c[:, :], csc_d[:, :], inc="dS", incn=16); bump("dS", 16)
        E("sync", "dma_start", csb_s[:, :], css_d[:, :], inc="dS", incn=16); dS_cs = bump("dS", 16)
        E("sync", "dma_start", mask01[:, :, :], mask_d[:, :, :], inc="dS", incn=16); dS_mask = bump("dS", 16)
        E("sync", "dma_start", biasqk_r[:, :], wqkvb_d.rearrange("(a p) -> a p", p=128),
          inc="dS", incn=16); dS_bqk = bump("dS", 16)

        E("gpsimd", "dma_start", vbias_row[:, :], wqkvb_d[2 * D:3 * D].unsqueeze(0), inc="dG", incn=16)
        dG_vb = bump("dG", 16)
        E("gpsimd", "dma_start", wa_bf[:, :, :], wa_d.rearrange("(a p) e -> p a e", p=128),
          inc="dG", incn=16); dG_wa = bump("dG", 16)
        E("gpsimd", "dma_start", rb2_bf[:, :], rb2_d[:, :], inc="dG", incn=16); bump("dG", 16)
        E("gpsimd", "dma_start", sb2_bf[:, :], sb2_d[:, :], inc="dG", incn=16); dG_b2 = bump("dG", 16)
        dG_wq = []
        for i in range(12):
            E("gpsimd", "dma_start", wq_bf[i * 512:(i + 1) * 512, :], wqkv_d[i * 512:(i + 1) * 512, :],
              inc="dG", incn=16)
            dG_wq.append(bump("dG", 16))

        E("tensor", "transpose", P_D[:, 0:48], biasqk_r[:, :], ident[0:48, 0:48],
          w=[("dS", dS_bqk), ("dS", dS_ident)], inc="cT")
        fill["PD0"] = ("cT", bump("cT"))
        E("vector", "tensor_copy", biasqk[:, :], P_D[:, 0:48], w=ge("PD0"), inc="cV")
        evac["PD0"] = ("cV", bump("cV"))
        E("vector", "tensor_scalar", biasqk[:, 16:32], biasqk[:, 16:32], SCALE, None, ALU.mult,
          w=[("cV", CNT["cV"])], inc="cV")
        cv_biasqk = bump("cV")

        xrel = [0, 0]
        xnb_rel = [0, 0]
        for tc in range(8):
            b = tc % 2
            E("sync", "dma_start", x_t[b], x_d[tc * 128:(tc + 1) * 128, :],
              w=[("cV", xrel[b])], inc="dS", incn=16)
            dS_x = bump("dS", 16)
            E("vector", "tensor_reduce", topk_s[:, 0:1], x_t[b], AX, ALU.add,
              w=[("dS", dS_x)], inc="cV")
            cV_sum = bump("cV")
            E("scalar", "activation", ln_scr, x_t[b], AF.Square, accum_out=topk_s[:, 1:2],
              w=[("dS", dS_x), ("cV", cV_sum)], inc="cA")
            cA_sq = bump("cA")
            E("vector", "tensor_scalar", topk_s[:, 2:3], topk_s[:, 0:1], 1.0 / D, None, ALU.mult,
              w=[("cA", cA_sq)], inc="cV")
            cv_e = bump("cV")
            E("vector", "tensor_scalar", topk_s[:, 3:4], topk_s[:, 1:2], 1.0 / D, None, ALU.mult,
              w=[("cV", cv_e)], inc="cV")
            cv_e = bump("cV")
            E("vector", "tensor_tensor", topk_s[:, 4:5], topk_s[:, 2:3], topk_s[:, 2:3], ALU.mult,
              w=[("cV", cv_e)], inc="cV")
            cv_e = bump("cV")
            E("vector", "tensor_tensor", topk_s[:, 3:4], topk_s[:, 3:4], topk_s[:, 4:5], ALU.subtract,
              w=[("cV", cv_e)], inc="cV")
            cv_e = bump("cV")
            E("vector", "tensor_scalar", topk_s[:, 3:4], topk_s[:, 3:4], 1.0, EPS, ALU.mult, ALU.add,
              w=[("cV", cv_e)], inc="cV")
            cV_var = bump("cV")
            E("scalar", "activation", topk_s[:, 5:6], topk_s[:, 3:4], AF.Ln, w=[("cV", cV_var)],
              inc="cA")
            ca_e = bump("cA")
            E("scalar", "activation", topk_s[:, 6:7], topk_s[:, 5:6], AF.Exp, scale=-0.5,
              w=[("cA", ca_e)], inc="cA")
            cA_rstd = bump("cA")
            E("vector", "tensor_scalar", xn_bf_t[b], x_t[b], topk_s[:, 2:3], topk_s[:, 6:7],
              ALU.subtract, ALU.mult, w=[("cA", cA_rstd), ("dS", xnb_rel[b])], inc="cV")
            cV_xnb = bump("cV")
            E("vector", "tensor_scalar", out_acc[:, tc, :], x_t[b], topk_s[:, 2:3], topk_s[:, 6:7],
              ALU.subtract, ALU.mult, inc="cV")
            xrel[b] = bump("cV")
            E("sync", "dma_start", xn_dram[tc * 128:(tc + 1) * 128, :], xn_bf_t[b],
              w=[("cV", cV_xnb)], inc="dS", incn=16)
            xnb_rel[b] = bump("dS", 16)
        dS_xnd = CNT["dS"]
        cV_ln_done = CNT["cV"]
        cA_ln_done = CNT["cA"]

        for dc in range(DC):
            E("sync", "dma_start", xn_fm[:, dc, :], xn_dram[:, dc * 128:(dc + 1) * 128],
              w=[("dS", dS_xnd)] if dc == 0 else [], transpose=True, inc="dS", incn=16)
            bump("dS", 16)
        dS_xnfm = CNT["dS"]

        wqT_loaded = [0, 0]
        wqT_free = [0, 0]

        def load_wqT(og, buf):
            for dc in range(DC):
                E("sync", "dma_start", wqT[buf][:, dc, :],
                  wq_bf[og * 512:(og + 1) * 512, dc * 128:(dc + 1) * 128],
                  w=([("dG", dG_wq[og]), ("cT", wqT_free[buf])] if dc == 0 else []),
                  transpose=True, inc="dS", incn=16)
                wqT_loaded[buf] = bump("dS", 16)

        load_wqT(0, 0)
        evrel = [0, 0, 0, 0]
        for og in range(12):
            buf = og % 2
            if og + 1 < 12:
                load_wqT(og + 1, 1 - buf)
            if og < 8:
                for oc2 in range(4):
                    oc = og * 4 + oc2
                    is_k = oc >= 16
                    wts = gf("PA0", "PA1") + [("dS", wqT_loaded[buf]), ("dS", dS_xnfm)]
                    for dc in range(DC):
                        for half in range(2):
                            E("tensor", "matmul", P_A[:, half * 512:(half + 1) * 512],
                              wqT[buf][:, dc, oc2 * 128:(oc2 + 1) * 128],
                              xn_fm[:, dc, half * 512:(half + 1) * 512],
                              w=wts, start=(dc == 0), stop=(dc == DC - 1),
                              inc="cT" if (half == 1 and dc == DC - 1) else None)
                            wts = []
                    fill["PA0"] = fill["PA1"] = ("cT", bump("cT"))
                    if oc2 == 3:
                        wqT_free[buf] = CNT["cT"]
                    ev = qkv_ev[oc % 4]
                    E("scalar", "activation", ev, P_A[:, :], AF.Identity,
                      bias=biasqk[:, oc:oc + 1], scale=SCALE if is_k else 1.0,
                      w=ge("PA0") + [("cV", cv_biasqk), ("dS", evrel[oc % 4])], inc="cA")
                    evac["PA0"] = evac["PA1"] = ("cA", bump("cA"))
                    cA_ev = CNT["cA"]
                    E("vector", "stream_shuffle", rope_scr[0:32, 0, :], ev[0:32, :],
                      list(range(16, 32)) + list(range(16)),
                      w=[("cA", cA_ev), ("dS", dS_cs)], inc="cV")
                    cv_e = bump("cV")
                    E("vector", "tensor_tensor", rope_scr[0:32, 0, :], rope_scr[0:32, 0, :],
                      csb_s[:, :], ALU.mult, w=[("cV", cv_e)], inc="cV")
                    cv_e = bump("cV")
                    E("vector", "tensor_tensor", ev[0:32, :], ev[0:32, :], csb_c[:, :], ALU.mult,
                      inc="cV")
                    cv_e = bump("cV")
                    E("vector", "tensor_tensor", ev[0:32, :], ev[0:32, :], rope_scr[0:32, 0, :],
                      ALU.add, w=[("cV", cv_e)], inc="cV")
                    cV_rope = bump("cV")
                    dst = kc_dram if is_k else q_dram
                    row0 = ((oc - 16) if is_k else oc) * 128
                    E("sync", "dma_start", dst[row0:row0 + 128, :], ev,
                      w=[("cV", cV_rope)], inc="dS", incn=16)
                    evrel[oc % 4] = bump("dS", 16)
            else:
                ogv = og - 8
                for tc in range(8):
                    pk = "PB0" if tc % 2 == 0 else "PB1"
                    half = P_B[:, 0:512] if tc % 2 == 0 else P_B[:, 512:1024]
                    wts = gf(pk) + [("dS", wqT_loaded[buf]), ("dG", dG_vb), ("cV", cv_consts)]
                    for dc in range(DC):
                        E("tensor", "matmul", half, xn_fm[:, dc, tc * 128:(tc + 1) * 128],
                          wqT[buf][:, dc, :], w=wts, start=(dc == 0), stop=False)
                        wts = []
                    E("tensor", "matmul", half, onesr_bf[0:1, :],
                      vbias_row[0:1, ogv * 512:(ogv + 1) * 512], start=False, stop=True, inc="cT")
                    fill[pk] = ("cT", bump("cT"))
                    if tc >= 6:
                        wqT_free[buf] = CNT["cT"]
                    ev = qkv_ev[tc % 2]
                    E("scalar", "activation", ev[:, 0:512], half, AF.Copy,
                      w=ge(pk) + [("dS", evrel[tc % 2])], inc="cA")
                    evac[pk] = ("cA", bump("cA"))
                    E("sync", "dma_start",
                      vc_dram[tc * 128:(tc + 1) * 128, ogv * 512:(ogv + 1) * 512], ev[:, 0:512],
                      w=[("cA", CNT["cA"])], inc="dS", incn=16)
                    evrel[tc % 2] = bump("dS", 16)
        dS_contrib = CNT["dS"]
        cT_qkv = CNT["cT"]

        E("gpsimd", "collective_compute", "AllGather", ALU.bypass,
          replica_groups=[list(range(NCORE))], ins=[kc_dram[:, :].opt()],
          outs=[kg_dram[:, :].opt()], w=[("dS", dS_contrib)], inc="cC")
        bump("cC")
        E("gpsimd", "collective_compute", "AllGather", ALU.bypass,
          replica_groups=[list(range(NCORE))], ins=[vc_dram[:, :].opt()],
          outs=[vg_dram[:, :].opt()], inc="cC")
        cC_done = bump("cC")

        E("gpsimd", "dma_start", rgb_sb, rgb_d[:, :], w=[("cV", cV_ln_done)], inc="dG", incn=16)
        bump("dG", 16)
        E("gpsimd", "dma_start", rb1_sb, rb1_d[:, :], inc="dG", incn=16); bump("dG", 16)
        E("gpsimd", "dma_start", sgb_sb, sgb_d[:, :], inc="dG", incn=16); bump("dG", 16)
        E("gpsimd", "dma_start", sb1_sb, sb1_d[:, :], inc="dG", incn=16)
        dG_bias = bump("dG", 16)

        for (src_t, dstc, n) in ((rgb_sb, bcol_g, 16), (rb1_sb, bcol_w, 16),
                                 (sgb_sb, bcol_gs, 2), (sb1_sb, bcol_ws, 2)):
            for fc in range(FC):
                E("tensor", "transpose", P_D[:, 0:n], src_t[0:n, fc * 128:(fc + 1) * 128],
                  ident[0:n, 0:n], w=gf("PD0") + [("dG", dG_bias)], inc="cT")
                fill["PD0"] = ("cT", bump("cT"))
                E("vector", "tensor_copy", dstc[:, fc, 0:n], P_D[:, 0:n], w=ge("PD0"), inc="cV")
                evac["PD0"] = ("cV", bump("cV"))
        cT_biasT = CNT["cT"]
        cV_biasT = CNT["cV"]

        wts = gf("PD0", "PD1") + [("dG", dG_wa), ("dS", dS_xnfm)]
        for half in range(2):
            for dc in range(DC):
                E("tensor", "matmul", P_D[0:16, half * 512:(half + 1) * 512], wa_bf[:, dc, :],
                  xn_fm[:, dc, half * 512:(half + 1) * 512], w=wts, start=(dc == 0),
                  stop=(dc == DC - 1), inc="cT" if (half == 1 and dc == DC - 1) else None)
                wts = []
        fill["PD0"] = fill["PD1"] = ("cT", bump("cT"))
        E("scalar", "activation", aff_fm, P_D[0:16, :], AF.Sigmoid,
          w=ge("PD0") + [("cA", cA_ln_done), ("cV", cV_ln_done)], inc="cA")
        evac["PD0"] = evac["PD1"] = ("cA", bump("cA"))
        cA_aff = CNT["cA"]

        for tc in range(8):
            E("tensor", "transpose", P_D[:, 0:16], aff_fm[0:16, tc * 128:(tc + 1) * 128],
              ident[0:16, 0:16], w=gf("PD0") + [("cA", cA_aff)], inc="cT")
            fill["PD0"] = ("cT", bump("cT"))
            E("vector", "tensor_copy", aff_tm[:, tc, :], P_D[:, 0:16], w=ge("PD0"), inc="cV")
            evac["PD0"] = ("cV", bump("cV"))
            a = aff_tm[:, tc, :]
            E("vector", "tensor_reduce", topk_s[:, 0:1], a, AX, ALU.max, w=[("cV", CNT["cV"])],
              inc="cV")
            cv_e = bump("cV")
            E("vector", "tensor_single_scalar", comb_tm[:, tc, :], a, topk_s[:, 0:1], ALU.is_ge,
              w=[("cV", cv_e)], inc="cV")
            cv_e = bump("cV")
            E("vector", "scalar_tensor_tensor", comb_tm[:, tc, :], comb_tm[:, tc, :], -1e9, a,
              ALU.mult, ALU.add, w=[("cV", cv_e)], inc="cV")
            cv_e = bump("cV")
            E("vector", "tensor_reduce", topk_s[:, 1:2], comb_tm[:, tc, :], AX, ALU.max,
              w=[("cV", cv_e)], inc="cV")
            cv_e = bump("cV")
            E("vector", "tensor_tensor", topk_s[:, 2:3], topk_s[:, 0:1], topk_s[:, 1:2], ALU.add,
              w=[("cV", cv_e)], inc="cV")
            cV_den = bump("cV")
            E("scalar", "activation", topk_s[:, 3:4], topk_s[:, 2:3], AF.Ln, w=[("cV", cV_den)],
              inc="cA")
            ca_e = bump("cA")
            E("scalar", "activation", topk_s[:, 4:5], topk_s[:, 3:4], AF.Exp, scale=-1.0,
              w=[("cA", ca_e)], inc="cA")
            cA_rd = bump("cA")
            E("vector", "tensor_single_scalar", comb_tm[:, tc, :], a, topk_s[:, 1:2], ALU.is_ge,
              w=[("cA", cA_rd)], inc="cV")
            cv_e = bump("cV")
            E("vector", "tensor_tensor", comb_tm[:, tc, :], comb_tm[:, tc, :], a, ALU.mult,
              w=[("cV", cv_e)], inc="cV")
            cv_e = bump("cV")
            E("vector", "tensor_scalar", comb_tm[:, tc, :], comb_tm[:, tc, :], topk_s[:, 4:5], None,
              ALU.mult, w=[("cV", cv_e)], inc="cV")
            cV_ctm = bump("cV")
            E("tensor", "transpose", P_D[0:16, 512:640], comb_tm[:, tc, :], ident[:, :],
              w=gf("PD1") + [("cV", cV_ctm)], inc="cT")
            fill["PD1"] = ("cT", bump("cT"))
            E("vector", "tensor_copy", comb_fm[0:16, tc * 128:(tc + 1) * 128], P_D[0:16, 512:640],
              w=ge("PD1"), inc="cV")
            evac["PD1"] = ("cV", bump("cV"))
        E("vector", "tensor_copy", comb_bf[:, :], comb_fm, inc="cV")
        cV_comb = bump("cV")
        crow_guard = CNT["cT"]

        def wsrc(u):
            if u < NRE:
                return rg_d[u], rw1_d[u], rw2_d[u], bcol_g, bcol_w, u, True
            s = u - NRE
            return sg_d[s], sw1_d[s], sw2_d[s], bcol_gs, bcol_ws, s, False

        gw_loaded = [0, 0]
        gw_free = [0, 0]
        w2_loaded = 0
        w2_free = 0

        def load_gw(u, fc, buf, extra=()):
            nonlocal gw_loaded
            g_src, w1_src = wsrc(u)[0], wsrc(u)[1]
            first = True
            for (srcw, dstb) in ((g_src, gwg[buf]), (w1_src, gww[buf])):
                v = srcw.rearrange("(a p) f -> p a f", p=128)[:, :, fc * 128:(fc + 1) * 128]
                E("gpsimd", "dma_start", dstb[:, 0:8, :], v[:, 0:8, :],
                  w=([("cT", gw_free[buf])] + list(extra)) if first else [], inc="dG", incn=16)
                first = False
                bump("dG", 16)
                E("gpsimd", "dma_start", dstb[:, 8:16, :], v[:, 8:16, :], inc="dG", incn=16)
                gw_loaded[buf] = bump("dG", 16)

        def load_w2(u, extra=()):
            nonlocal w2_loaded
            w2_src = wsrc(u)[2]
            for fc in range(FC):
                E("gpsimd", "dma_start", w2buf[:, fc, :], w2_src[fc * 128:(fc + 1) * 128, :],
                  w=([("cT", w2_free)] + list(extra)) if fc == 0 else [], inc="dG", incn=16)
                w2_loaded = bump("dG", 16)

        load_gw(0, 0, 0, extra=[("cT", cT_qkv)])

        cV_tgw = 0
        cV_cbc_user = 0
        dG_crow = 0
        for u in range(NU):
            _, _, _, bg, bw, bi, routed = wsrc(u)
            if routed:
                E("gpsimd", "dma_start", crow[0:1, :], comb_fm[bi:bi + 1, :],
                  w=[("cV", cV_comb), ("cT", crow_guard)], inc="dG", incn=16)
                dG_crow = bump("dG", 16)
                E("tensor", "matmul", P_D[:, 0:512], ones_f[0:1, :], crow[0:1, 0:512],
                  w=gf("PD0", "PD1") + [("dG", dG_crow)], start=True, stop=True)
                E("tensor", "matmul", P_D[:, 512:1024], ones_f[0:1, :], crow[0:1, 512:1024],
                  start=True, stop=True, inc="cT")
                fill["PD0"] = fill["PD1"] = ("cT", bump("cT"))
                crow_guard = CNT["cT"]
                E("vector", "tensor_copy", c_bc, P_D[:, :],
                  w=ge("PD0") + [("cV", cV_cbc_user), ("cV", cV_biasT)], inc="cV")
                evac["PD0"] = evac["PD1"] = ("cV", bump("cV"))
            for fc in range(FC):
                buf = (u * FC + fc) % 2
                nfc, nu_ = fc + 1, u
                if nfc == FC:
                    nfc, nu_ = 0, u + 1
                if nu_ < NU:
                    load_gw(nu_, nfc, 1 - buf)
                if fc == 1:
                    load_w2(u, extra=[("dS", dS_contrib)] if u == 0 else ())
                wts = gf("PA0", "PA1") + [("dG", gw_loaded[buf]), ("dS", dS_xnfm)]
                for dc in range(DC):
                    for half in range(2):
                        E("tensor", "matmul", P_A[:, half * 512:(half + 1) * 512], gwg[buf][:, dc, :],
                          xn_fm[:, dc, half * 512:(half + 1) * 512], w=wts, start=(dc == 0),
                          stop=(dc == DC - 1), inc="cT" if (half == 1 and dc == DC - 1) else None)
                        wts = []
                fill["PA0"] = fill["PA1"] = ("cT", bump("cT"))
                wts = gf("PB0", "PB1")
                for dc in range(DC):
                    for half in range(2):
                        E("tensor", "matmul", P_B[:, half * 512:(half + 1) * 512], gww[buf][:, dc, :],
                          xn_fm[:, dc, half * 512:(half + 1) * 512], w=wts, start=(dc == 0),
                          stop=(dc == DC - 1), inc="cT" if (half == 1 and dc == DC - 1) else None)
                        wts = []
                fill["PB0"] = fill["PB1"] = ("cT", bump("cT"))
                gw_free[buf] = CNT["cT"]
                E("scalar", "activation", t_g, P_A[:, :], AF.Gelu, bias=bg[:, fc, bi:bi + 1],
                  w=ge("PA0") + [("cV", max(cV_tgw, cV_biasT))], inc="cA")
                evac["PA0"] = evac["PA1"] = ("cA", bump("cA"))
                E("scalar", "activation", t_w, P_B[:, :], AF.Identity, bias=bw[:, fc, bi:bi + 1],
                  w=ge("PB0"), inc="cA")
                evac["PB0"] = evac["PB1"] = ("cA", bump("cA"))
                cA_tgw = CNT["cA"]
                if routed:
                    E("vector", "tensor_tensor", t_g, t_g, t_w, ALU.mult, w=[("cA", cA_tgw)])
                    E("vector", "tensor_tensor", y_sb[:, fc, :], t_g, c_bc, ALU.mult, inc="cV")
                else:
                    E("vector", "tensor_tensor", y_sb[:, fc, :], t_g, t_w, ALU.mult,
                      w=[("cA", cA_tgw)], inc="cV")
                cV_tgw = bump("cV")
            cV_yu = CNT["cV"]
            cV_cbc_user = cV_yu
            last = (u == NU - 1)
            for tc in range(8):
                for wp2 in range(2):
                    wins = (2 * wp2, 2 * wp2 + 1)
                    wts = gf("PC0", "PC1") + [("dG", w2_loaded), ("cV", cV_yu)]
                    for fc in range(FC):
                        for (wi, win) in enumerate(wins):
                            half = P_C[:, 0:512] if wi == 0 else P_C[:, 512:1024]
                            stop = (fc == FC - 1) and not last
                            E("tensor", "matmul", half, y_sb[:, fc, tc * 128:(tc + 1) * 128],
                              w2buf[:, fc, win * 512:(win + 1) * 512], w=wts, start=(fc == 0),
                              stop=stop, inc="cT" if (stop and wi == 1) else None)
                            wts = []
                    if last:
                        for (wi, win) in enumerate(wins):
                            half = P_C[:, 0:512] if wi == 0 else P_C[:, 512:1024]
                            E("tensor", "matmul", half, comb_bf[0:16, tc * 128:(tc + 1) * 128],
                              rb2_bf[0:16, win * 512:(win + 1) * 512],
                              w=[("dG", dG_b2), ("cV", cV_comb)], start=False, stop=False)
                            E("tensor", "matmul", half, ones2_bf[0:2, :],
                              sb2_bf[0:2, win * 512:(win + 1) * 512], start=False, stop=True,
                              inc="cT" if wi == 1 else None)
                    fill["PC0"] = fill["PC1"] = ("cT", bump("cT"))
                    for (wi, win) in enumerate(wins):
                        half = P_C[:, 0:512] if wi == 0 else P_C[:, 512:1024]
                        E("vector", "tensor_tensor", out_acc[:, tc, win * 512:(win + 1) * 512], half,
                          out_acc[:, tc, win * 512:(win + 1) * 512], ALU.add,
                          w=ge("PC0") if wi == 0 else [], inc="cV" if wi == 1 else None)
                    evac["PC0"] = evac["PC1"] = ("cV", bump("cV"))
            w2_free = CNT["cT"]
        cV_ffn = CNT["cV"]
        cT_ffn = CNT["cT"]

        for tc in range(8):
            E("sync", "dma_start", ffn_dram[tc * 128:(tc + 1) * 128, :], out_acc[:, tc, :],
              w=[("cV", cV_ffn)] if tc == 0 else [], inc="dS", incn=16)
            bump("dS", 16)
        dS_park = CNT["dS"]

        E("sync", "dma_start", q_sb[:, :, :], q_dram.rearrange("(h p) t -> p h t", p=128),
          w=[("cT", cT_ffn)], inc="dS", incn=16)
        dS_qsb = bump("dS", 16)

        kg_v = kg_dram.rearrange("(r q) t -> r q t", r=NCORE)
        vg_v = vg_dram.rearrange("(r p2) d -> r p2 d", r=NCORE)
        kv_loaded = [0, 0]
        kv_free = [cT_ffn, cT_ffn]
        pt_free = [cT_ffn, cT_ffn]
        pt_vfree = [0, 0]
        lsum_mm = cT_ffn
        lacc_guard = dG_crow
        sacc_first = True

        def load_kv(w_, h_, b_):
            nonlocal kv_loaded
            E("sync", "dma_start", k_h[b_][:, :, :],
              kg_v[:, h_ * 128:(h_ + 1) * 128, w_ * 128:(w_ + 1) * 128].transpose([1, 0, 2]),
              w=[("cC", cC_done), ("cT", kv_free[b_])], inc="dS", incn=16)
            bump("dS", 16)
            E("sync", "dma_start", v_h[b_][:, :, :],
              vg_v[:, w_ * 128:(w_ + 1) * 128, h_ * 128:(h_ + 1) * 128].transpose([1, 0, 2]),
              inc="dS", incn=16)
            kv_loaded[b_] = bump("dS", 16)

        load_kv(0, 0, 0)
        for w_ in range(8):
            for h in range(H):
                b = (w_ * H + h) % 2
                if (w_, h) != (7, H - 1):
                    nw, nh = (w_, h + 1) if h + 1 < H else (w_ + 1, 0)
                    load_kv(nw, nh, 1 - b)
                for (pi, (c0, wp)) in enumerate(pieces_of(w_)):
                    pb = pi % 2
                    diag = (c0 == w_ * 128)
                    exp_cA = []
                    mask_cV = []
                    for r in range(NCORE):
                        sk = "PA0" if r % 2 == 0 else "PA1"
                        half = P_A[:, 0:wp] if r % 2 == 0 else P_A[:, 512:512 + wp]
                        wts = gf(sk) + ([("dS", kv_loaded[b]), ("dS", dS_qsb)] if r == 0 else [])
                        E("tensor", "matmul", half, k_h[b][:, r, :], q_sb[:, h, c0:c0 + wp],
                          w=wts, start=True, stop=True, inc="cT")
                        fill[sk] = ("cT", bump("cT"))
                        E("scalar", "activation", p_t[pb][:, r, 0:wp], half, AF.Exp,
                          bias=negc[:, 0:1],
                          w=ge(sk) + ([("cV", max(cv_consts, pt_vfree[pb])),
                                       ("cT", pt_free[pb])] if r == 0 else []),
                          inc="cA")
                        evac[sk] = ("cA", bump("cA"))
                        exp_cA.append(CNT["cA"])
                        if diag:
                            E("vector", "tensor_tensor", p_t[pb][:, r, 0:128], p_t[pb][:, r, 0:128],
                              mask01[:, r, :], ALU.mult,
                              w=[("cA", exp_cA[r])] + ([("dS", dS_mask)] if r == 0 else []),
                              inc="cV")
                            mask_cV.append(bump("cV"))
                    pk = "PC0" if pb == 0 else "PC1"
                    halfc = P_C[:, 0:wp] if pb == 0 else P_C[:, 512:512 + wp]
                    for r in range(NCORE):
                        wts = (gf(pk) if r == 0 else []) + [("cA", exp_cA[r])]
                        if diag:
                            wts.append(("cV", mask_cV[r]))
                        E("tensor", "matmul", halfc, v_h[b][:, r, :], p_t[pb][:, r, 0:wp],
                          w=wts, start=(r == 0), stop=(r == NCORE - 1),
                          inc="cT" if r == NCORE - 1 else None)
                    fill[pk] = ("cT", bump("cT"))
                    pt_free[pb] = CNT["cT"]
                    E("vector", "tensor_tensor", lsum_bf[:, 0:wp], p_t[pb][:, 0, 0:wp],
                      p_t[pb][:, 1, 0:wp], ALU.add,
                      w=[("cA", exp_cA[-1]), ("cT", lsum_mm)])
                    for r in range(2, NCORE):
                        E("vector", "tensor_tensor", lsum_bf[:, 0:wp], lsum_bf[:, 0:wp],
                          p_t[pb][:, r, 0:wp], ALU.add,
                          inc="cV" if r == NCORE - 1 else None)
                    cV_lsum = bump("cV")
                    pt_vfree[pb] = cV_lsum
                    if w_ == 0:
                        E("vector", "tensor_copy", s_acc[:, h, c0:c0 + wp], halfc,
                          w=ge(pk) + ([("dS", dS_park)] if sacc_first else []), inc="cV")
                        sacc_first = False
                    else:
                        E("vector", "tensor_tensor", s_acc[:, h, c0:c0 + wp], halfc,
                          s_acc[:, h, c0:c0 + wp], ALU.add, w=ge(pk), inc="cV")
                    evac[pk] = ("cV", bump("cV"))
                    E("tensor", "matmul", P_D[0:16, 0:wp], oneh[:, h, :], lsum_bf[:, 0:wp],
                      w=gf("PD0") + [("cV", cV_lsum), ("dS", dS_oneh)], start=True, stop=True,
                      inc="cT")
                    fill["PD0"] = ("cT", bump("cT"))
                    lsum_mm = CNT["cT"]
                    if w_ == 0 and h == 0:
                        E("vector", "tensor_copy", lacc16[:, c0:c0 + wp], P_D[0:16, 0:wp],
                          w=ge("PD0") + [("dG", lacc_guard)], inc="cV")
                    else:
                        E("vector", "tensor_tensor", lacc16[:, c0:c0 + wp], P_D[0:16, 0:wp],
                          lacc16[:, c0:c0 + wp], ALU.add, w=ge("PD0"), inc="cV")
                    evac["PD0"] = ("cV", bump("cV"))
                kv_free[b] = CNT["cT"]
        cV_att = CNT["cV"]

        E("scalar", "activation", lacc16, lacc16, AF.Ln, w=[("cV", cV_att)], inc="cA")
        ca_e = bump("cA")
        E("scalar", "activation", lacc16, lacc16, AF.Exp, scale=-1.0, w=[("cA", ca_e)], inc="cA")
        cA_rl = bump("cA")
        crow_free = crow_guard
        for h in range(H):
            E("gpsimd", "dma_start", crow[0:1, :], lacc16[h:h + 1, :],
              w=[("cA", cA_rl), ("cT", crow_free)], inc="dG", incn=16)
            dG_lrow = bump("dG", 16)
            E("tensor", "matmul", P_D[:, 0:512], ones_f[0:1, :], crow[0:1, 0:512],
              w=gf("PD0", "PD1") + [("dG", dG_lrow)], start=True, stop=True)
            E("tensor", "matmul", P_D[:, 512:1024], ones_f[0:1, :], crow[0:1, 512:1024],
              start=True, stop=True, inc="cT")
            fill["PD0"] = fill["PD1"] = ("cT", bump("cT"))
            crow_free = CNT["cT"]
            E("vector", "tensor_tensor", s_acc[:, h, :], P_D[:, :], s_acc[:, h, :], ALU.mult,
              w=ge("PD0"), inc="cV")
            evac["PD0"] = evac["PD1"] = ("cV", bump("cV"))
        cV_norm = CNT["cV"]

        xh_rel = ("cV", cV_ffn)
        fh_rel = ("cV", cV_ffn)
        sh_rel = ("cV", 0)
        gh_rel = ("cV", cV_ffn)
        for tc in range(8):
            for dh in range(2):
                for grp in range(2):
                    pk = "PA0" if grp == 0 else "PA1"
                    half = P_A[:, 0:512] if grp == 0 else P_A[:, 512:1024]
                    wts = gf(pk) + [("cV", cV_norm)]
                    for j in range(4):
                        h = dh * 8 + grp * 4 + j
                        E("tensor", "transpose", half[:, j * 128:(j + 1) * 128],
                          s_acc[:, h, tc * 128:(tc + 1) * 128], ident[:, :],
                          w=wts, inc="cT" if j == 3 else None)
                        wts = []
                    fill[pk] = ("cT", bump("cT"))
                    E("vector", "tensor_copy", s_h[:, grp * 512:(grp + 1) * 512], half,
                      w=ge(pk) + ([sh_rel] if grp == 0 else []), inc="cV")
                    evac[pk] = ("cV", bump("cV"))
                cV_sh = CNT["cV"]
                E("sync", "dma_start", x_h, x_d[tc * 128:(tc + 1) * 128, dh * 1024:(dh + 1) * 1024],
                  w=[xh_rel], inc="dS", incn=16)
                dS_xh = bump("dS", 16)
                E("sync", "dma_start", f_h, ffn_dram[tc * 128:(tc + 1) * 128, dh * 1024:(dh + 1) * 1024],
                  w=[("dS", dS_park), fh_rel], inc="dS", incn=16)
                dS_fh = bump("dS", 16)
                E("scalar", "activation", g_h, f_h, AF.Silu,
                  w=[("dS", dS_fh), gh_rel], inc="cA")
                cA_g = bump("cA")
                fh_rel = ("cA", cA_g)
                E("vector", "tensor_tensor", g_h, g_h, s_h, ALU.mult,
                  w=[("cA", cA_g), ("cV", cV_sh)], inc="cV")
                cV_gm = bump("cV")
                sh_rel = ("cV", cV_gm)
                E("vector", "tensor_tensor", x_h, x_h, g_h, ALU.add,
                  w=[("dS", dS_xh)], inc="cV")
                cV_out = bump("cV")
                gh_rel = ("cV", cV_out)
                E("sync", "dma_start", y_d[tc * 128:(tc + 1) * 128, dh * 1024:(dh + 1) * 1024],
                  x_h, w=[("cV", cV_out)], inc="dS", incn=16)
                xh_rel = ("dS", bump("dS", 16))
        if debug:
            taps = [("dbg_xn", xn_dram, None), ("dbg_q", q_dram, None), ("dbg_kc", kc_dram, None),
                    ("dbg_vc", vc_dram, None), ("dbg_ffn", ffn_dram, None)]
            for (nm, srcd, _) in taps:
                td = nc.dram_tensor(nm, list(srcd.tensor.shape), srcd.dtype, kind="ExternalOutput").ap()
                E("sync", "dma_start", td[:, :], srcd[:, :], w=[("dS", CNT["dS"])], inc="dS", incn=16)
                bump("dS", 16)
            for (nm, ssrc, shape) in (("dbg_comb", comb_fm, [16, 1024]),
                                      ("dbg_afftm", aff_tm[:, :, :].rearrange("p a c -> p (a c)"), [128, 128]),
                                      ("dbg_combtm", comb_tm[:, :, :].rearrange("p a c -> p (a c)"), [128, 128]),
                                      ("dbg_l", lacc16, [16, 1024]),
                                      ("dbg_sacc", B32[:, :], [128, 16384])):
                td = nc.dram_tensor(nm, shape, F32, kind="ExternalOutput").ap()
                E("sync", "dma_start", td[:, :], ssrc, w=[("dS", CNT["dS"]), ("cV", CNT["cV"])],
                  inc="dS", incn=16)
                bump("dS", 16)
        dS_final = CNT["dS"]
        ST["sync"].append(lambda e, v=dS_final: e.wait_ge(SEM["dS"], v))

        @block.tensor
        def _(te):
            for f in ST["tensor"]:
                f(te)

        @block.vector
        def _(ve):
            for f in ST["vector"]:
                f(ve)

        @block.scalar
        def _(se):
            for f in ST["scalar"]:
                f(se)

        @block.sync
        def _(sy):
            for f in ST["sync"]:
                f(sy)

        @block.gpsimd
        def _(gp):
            for f in ST["gpsimd"]:
                f(gp)

    return nc


def _host_consts(c):
    tau = np.arange(T)
    pos = (8 * (tau // 128) + c) * 128 + (tau % 128)
    inv_freq = (10000.0 ** (-(np.arange(0, 32, 2, dtype=np.float32) / 32.0))).astype(np.float32)
    freqs = pos[:, None].astype(np.float32) * inv_freq[None, :]
    cosv = np.cos(freqs).T.astype(np.float32)
    sinv = np.sin(freqs).T.astype(np.float32)
    cs_c = np.concatenate([cosv, cosv], axis=0).astype(ml_dtypes.bfloat16)
    cs_s = np.concatenate([-sinv, sinv], axis=0).astype(ml_dtypes.bfloat16)
    ik = np.arange(128)[:, None]
    iq = np.arange(128)[None, :]
    m = np.zeros((128, 8, 128), np.float32)
    for r in range(8):
        if r < c:
            m[:, r, :] = 1.0
        elif r == c:
            m[:, r, :] = (ik <= iq).astype(np.float32)
    mask01 = m.astype(ml_dtypes.bfloat16)
    ident = np.eye(128, dtype=np.float32)
    return cs_c, cs_s, mask01, ident


def kernel(L0_attn_x, L0_attn_Wqkv_w, L0_attn_Wqkv_b, L0_ffn_wa, L0_ffn_rg, L0_ffn_rgb,
           L0_ffn_rw1, L0_ffn_rb1, L0_ffn_rw2, L0_ffn_rb2, L0_ffn_sg, L0_ffn_sgb,
           L0_ffn_sw1, L0_ffn_sb1, L0_ffn_sw2, L0_ffn_sb2):
    f32 = np.float32
    x = np.ascontiguousarray(np.asarray(L0_attn_x, f32).reshape(8192, D))
    shared = {
        "wqkv": np.ascontiguousarray(np.asarray(L0_attn_Wqkv_w, f32)),
        "wqkv_b": np.ascontiguousarray(np.asarray(L0_attn_Wqkv_b, f32)),
        "wa": np.ascontiguousarray(np.asarray(L0_ffn_wa, f32)),
        "rg": np.ascontiguousarray(np.asarray(L0_ffn_rg, f32)),
        "rgb": np.ascontiguousarray(np.asarray(L0_ffn_rgb, f32)),
        "rw1": np.ascontiguousarray(np.asarray(L0_ffn_rw1, f32)),
        "rb1": np.ascontiguousarray(np.asarray(L0_ffn_rb1, f32)),
        "rw2": np.ascontiguousarray(np.asarray(L0_ffn_rw2, f32)),
        "rb2": np.ascontiguousarray(np.asarray(L0_ffn_rb2, f32)),
        "sg": np.ascontiguousarray(np.asarray(L0_ffn_sg, f32)),
        "sgb": np.ascontiguousarray(np.asarray(L0_ffn_sgb, f32)),
        "sw1": np.ascontiguousarray(np.asarray(L0_ffn_sw1, f32)),
        "sb1": np.ascontiguousarray(np.asarray(L0_ffn_sb1, f32)),
        "sw2": np.ascontiguousarray(np.asarray(L0_ffn_sw2, f32)),
        "sb2": np.ascontiguousarray(np.asarray(L0_ffn_sb2, f32)),
    }
    x_tiles = x.reshape(64, 128, D)
    in_maps = []
    for c in range(NCORE):
        cs_c, cs_s, mask01, ident = _host_consts(c)
        m = dict(shared)
        m["x_sh"] = np.ascontiguousarray(x_tiles[c::8].reshape(T, D))
        m["cs_c"] = cs_c
        m["cs_s"] = cs_s
        m["mask01"] = mask01
        m["ident"] = ident
        m["oneh16"] = np.broadcast_to(np.eye(16, dtype=ml_dtypes.bfloat16)[None, :, :],
                                      (128, 16, 16)).copy()
        in_maps.append(m)

    nc = build_nc()
    res = run_bass_kernel_spmd(nc, in_maps, core_ids=list(range(NCORE)))

    out = np.empty((8192, D), f32)
    ot = out.reshape(64, 128, D)
    for c in range(NCORE):
        ot[c::8] = np.asarray(res.results[c]["y"]).reshape(8, 128, D)
    return out.reshape(1, 8192, D)
